# version 12 (final)
# speedup vs baseline: 1.3506x; 1.0273x over previous
import contextlib
import numpy as np
import ml_dtypes
import concourse.bass as bass
import concourse.mybir as mybir
from concourse.bass_utils import run_bass_kernel_spmd

F32 = mybir.dt.float32
BF16 = mybir.dt.bfloat16
AF = mybir.ActivationFunctionType
ALU = mybir.AluOpType
BF = ml_dtypes.bfloat16

ENGINES = ("sync", "tensor", "vector", "scalar", "gpsimd")
ENG_SEM = {"tensor": "PE", "vector": "DVE", "scalar": "ACT", "gpsimd": "GP"}

D = 1024
H = 16
DK = 64
L = 128
FF = 4096


class Em:

    def __init__(self, sems):
        self.sems = sems
        self.cnt = {k: 0 for k in sems}
        self.q = {e: [] for e in ENGINES}
        self.hi = {e: {k: 0 for k in sems} for e in ENGINES}

    def wait(self, eng, tok):
        if tok is None:
            return
        s, v = tok
        if v <= 0:
            return
        if ENG_SEM.get(eng) == s:
            return
        if v <= self.hi[eng][s]:
            return
        self.hi[eng][s] = v
        self.q[eng].append(("w", s, v))

    def waits(self, eng, toks):
        for t in toks:
            self.wait(eng, t)

    def op(self, eng, fn, inc=None, amt=1):
        self.q[eng].append(("c", fn, inc, amt))
        if inc:
            self.cnt[inc] += amt
            return (inc, self.cnt[inc])
        return None

    def tok(self, sem):
        return (sem, self.cnt[sem])

    def drain(self, eng):
        self.q[eng].append(("c", lambda e: e.drain(), None, 1))

    def last_inc(self, eng, sem, amt=1):
        for i in range(len(self.q[eng]) - 1, -1, -1):
            it = self.q[eng][i]
            if it[0] == "c":
                assert it[2] is None, "op already has an inc"
                self.q[eng][i] = ("c", it[1], sem, amt)
                self.cnt[sem] += amt
                return (sem, self.cnt[sem])
        raise RuntimeError("no op to attach inc")

    def flush(self, block):
        for eng in ENGINES:
            ops = self.q[eng]
            if not ops:
                continue

            def body(e, ops=ops, em=self):
                for it in ops:
                    if it[0] == "w":
                        e.wait_ge(em.sems[it[1]], it[2])
                    else:
                        inst = it[1](e)
                        if it[2]:
                            inst.then_inc(em.sems[it[2]], it[3])

            getattr(block, eng)(body)
            self.q[eng] = []


class Bank:

    def __init__(self, em, ap):
        self.em = em
        self.ap = ap
        self.tok = None
        self.fresh = True

    def acquire(self):
        self.em.wait("tensor", self.tok)
        self.fresh = True

    def start(self):
        st = self.fresh
        self.fresh = False
        return st

    def consumed(self, tok):
        self.tok = tok


class Slot:

    def __init__(self, em):
        self.em = em
        self.tok = None

    def wait_free(self, eng):
        self.em.wait(eng, self.tok)

    def read_by(self, tok):
        self.tok = tok


def build(T=4096, debug=False):
    CH = T // L
    NT = CH // 2
    nc = bass.Bass("TRN2", target_bir_lowering=False)

    scratch_kind = "ExternalOutput" if debug else "Internal"
    dx = nc.dram_tensor("x", [T, D], F32, kind="ExternalInput")
    dy = nc.dram_tensor("y", [T, D], F32, kind="ExternalOutput")
    dyp = nc.dram_tensor("ypart", [T, D], BF16, kind=scratch_kind)
    dw1s = nc.dram_tensor("w1scr", [FF, D], BF16, kind=scratch_kind)
    dw2s = nc.dram_tensor("w2scr", [D, FF], BF16, kind=scratch_kind)
    wnames = ["wq", "wk", "wv", "wgk", "wgc", "wo"]
    dws = {n: nc.dram_tensor(n, [D, D], F32, kind="ExternalInput") for n in wnames}
    dw1 = nc.dram_tensor("w1", [FF, D], F32, kind="ExternalInput")
    db1 = nc.dram_tensor("b1", [FF], F32, kind="ExternalInput")
    dw2 = nc.dram_tensor("w2", [D, FF], F32, kind="ExternalInput")
    db2 = nc.dram_tensor("b2", [D], F32, kind="ExternalInput")
    dbgk = nc.dram_tensor("bgk", [D], F32, kind="ExternalInput")
    dbgc = nc.dram_tensor("bgc", [D], F32, kind="ExternalInput")
    didb = nc.dram_tensor("identb", [128, 128], BF16, kind="ExternalInput")
    didf = nc.dram_tensor("identf", [128, 128], F32, kind="ExternalInput")
    dmi = nc.dram_tensor("maski", [128, 128], BF16, kind="ExternalInput")
    dms = nc.dram_tensor("masks", [128, 128], BF16, kind="ExternalInput")
    dones = nc.dram_tensor("onesrow", [1, 128], BF16, kind="ExternalInput")

    with contextlib.ExitStack() as ctx:
        sems = {}
        for s in ("QC", "QX0", "QX1", "QX2", "QX3", "QO0", "QO1", "QY0", "QY1",
                  "QW", "QW2", "QG", "QG0", "QG1", "QL0", "QL1", "PE", "DVE",
                  "ACT", "GP"):
            sems[s] = ctx.enter_context(nc.semaphore(s))
        em = Em(sems)

        sb = lambda name, shape, dt: ctx.enter_context(nc.sbuf_tensor(name, shape, dt))
        identb = sb("identb_s", [128, 128], BF16)
        identf = sb("identf_s", [128, 128], F32)
        maski = sb("maski_s", [128, 128], BF16)
        masks = sb("masks_s", [128, 128], BF16)
        onesr = sb("ones_s", [1, 128], BF16)
        SCG = sb("SCG", [128, 8 * 192], BF16)
        EAst = sb("EAst", [128, 8], F32)
        ERst = sb("ERst", [128, 8], F32)
        bgk_hi = sb("bgk_hi", [1, D], BF16)
        bgk_lo = sb("bgk_lo", [1, D], BF16)
        bgc_hi = sb("bgc_hi", [1, D], BF16)
        bgc_lo = sb("bgc_lo", [1, D], BF16)
        b2_hi = sb("b2_hi", [1, D], BF16)
        b1_sb = sb("b1_sb", [128, 32], F32)
        b1_st = sb("b1_st", [32, 128], F32)
        lnkeep = sb("lnkeep", [128, 2 * CH], F32)
        cbias = sb("cbias", [128, 2], F32)
        lntmp = sb("lntmp", [128, 8], F32)
        accm2 = sb("accm2", [128, 64], F32)
        ttmp = sb("ttmp", [128, 32], F32)
        ttr2 = sb("ttr2", [32, 128], F32)
        scs2 = sb("scs2", [2, 8], F32)
        rowst = sb("rowst", [32, 128], F32)
        colst = sb("colst", [128, 128], F32)

        ps = lambda name, shape, dt: ctx.enter_context(nc.psum_tensor(name, shape, dt))
        pF = ps("pF", [128, 1024], F32)
        pL = ps("pL", [128, 1024], F32)
        pA0 = ps("pA0", [128, 512], F32)
        pA1 = ps("pA1", [128, 512], F32)
        pB0 = ps("pB0", [128, 512], F32)
        pB1 = ps("pB1", [128, 512], F32)

        bF = [Bank(em, None), Bank(em, None)]
        bFoff = [0, 512]
        bL0 = Bank(em, None)
        bL1 = Bank(em, None)
        bA = [Bank(em, pA0), Bank(em, pA1)]
        bB = [Bank(em, pB0), Bank(em, pB1)]

        def dma(eng, out, in_, q):
            return em.op(eng, lambda e, o=out, i=in_: e.dma_start(o, i), inc=q, amt=16)

        tq = dma("sync", identb[:, :], didb[:, :], "QC")
        dma("sync", identf[:, :], didf[:, :], "QC")
        dma("sync", maski[:, :], dmi[:, :], "QC")
        dma("sync", masks[:, :], dms[:, :], "QC")
        dma("sync", onesr[:, :], dones[:, :], "QC")
        tok_c = em.tok("QC")
        em.q["sync"].append(("w", "QC", tok_c[1]))
        em.hi["sync"]["QC"] = tok_c[1]
        bg_f = sb("bg_f", [1, D], F32)
        dma("sync", bg_f[:, :], dbgk.ap()[None, :], "QC")
        em.wait("scalar", em.tok("QC"))
        t_hi = em.op("scalar", lambda e: e.activation(bgk_hi[:, :], bg_f[:, :], AF.Copy), inc="ACT")
        em.wait("vector", t_hi)
        t_lo = em.op("vector", lambda e: e.tensor_sub(bgk_lo[:, :], bg_f[:, :], bgk_hi[:, :]), inc="DVE")
        em.wait("sync", t_lo)
        dma("sync", bg_f[:, :], dbgc.ap()[None, :], "QC")
        em.wait("scalar", em.tok("QC"))
        t_hi = em.op("scalar", lambda e: e.activation(bgc_hi[:, :], bg_f[:, :], AF.Copy), inc="ACT")
        em.wait("vector", t_hi)
        t_lo = em.op("vector", lambda e: e.tensor_sub(bgc_lo[:, :], bg_f[:, :], bgc_hi[:, :]), inc="DVE")
        em.wait("sync", t_lo)
        dma("sync", bg_f[:, :], db2.ap()[None, :], "QC")
        em.wait("scalar", em.tok("QC"))
        tb2c = em.op("scalar", lambda e: e.activation(b2_hi[:, :], bg_f[:, :], AF.Copy), inc="ACT")
        em.wait("sync", tb2c)
        dma("sync", b1_st[:, :], db1.ap().rearrange("(c p) -> c p", p=128), "QC")
        em.wait("vector", em.tok("QC"))
        for b in range(4):
            tb1 = em.op("vector",
                        lambda e, b=b: e.transpose(b1_sb[32 * b:32 * b + 32, 0:32],
                                                   b1_st[0:32, 32 * b:32 * b + 32]),
                        inc="DVE")
        def S_slc(p, q):
            return SCG[64 * q:64 * (q + 1), 192 * p:192 * p + 64]

        def C_slc(p, q):
            return SCG[64 * q:64 * (q + 1), 192 * p + 64:192 * p + 128]

        def G_slc(p, q):
            return SCG[64 * q:64 * (q + 1), 192 * p + 128:192 * p + 192]

        em.op("vector", lambda e: e.memset(cbias[:, 0:1], 1.0), inc="DVE")
        em.op("vector", lambda e: e.memset(cbias[:, 1:2], 1e-5), inc="DVE")
        tok_cbias = em.tok("DVE")
        em.wait("scalar", tok_cbias)
        tms = em.op("vector", lambda e: e.memset(SCG[:, :], 0.0), inc="DVE")
        tok_states0 = em.tok("DVE")
        tok_consts = tok_c


        with contextlib.ExitStack() as ctxA:
            sbA = lambda name, shape, dt: ctxA.enter_context(nc.sbuf_tensor(name, shape, dt))
            W = {n: sbA(n + "_sb", [128, 8 * D], BF16) for n in wnames}
            x_sb = sbA("x_sb", [128, 2 * D], F32)
            sqsc = sbA("sqsc", [128, D], BF16)
            xn_tm = sbA("xn_tm", [128, 2 * D], BF16)
            xn_fm = sbA("xn_fm", [128, 8 * 256], BF16)
            Q_fm = sbA("Q_fm", [128, 2 * D], BF16)
            K_fm = sbA("K_fm", [128, 2 * D], BF16)
            V_tm = sbA("V_tm", [128, 2 * D], BF16)
            e_scr = sbA("e_scr", [128, 512], F32)
            spA = sbA("spA", [128, D], BF16)
            spR = sbA("spR", [128, D], BF16)
            EAx = sbA("EAx", [128, D], BF16)
            IAx = sbA("IAx", [128, D], BF16)
            IRx = sbA("IRx", [128, D], BF16)
            ERx = sbA("ERx", [128, D], BF16)
            Xs = sbA("Xs", [128, D], BF16)
            Kps = sbA("Kps", [128, D], BF16)
            Zs = sbA("Zs", [128, D], BF16)
            Kpns = sbA("Kpns", [128, D], BF16)
            PRm = sbA("PRm", [128, D], BF16)
            Kptm = sbA("Kptm", [128, D], BF16)
            Ztm = sbA("Ztm", [128, D], BF16)
            U_sb = sbA("U_sb", [128, 4 * 128], BF16)
            Bs_sb = sbA("Bs_sb", [128, 2 * 128], BF16)
            Mt_sb = sbA("Mt_sb", [128, 2 * 128], BF16)
            F_sb = sbA("F_sb", [128, 2 * 128], BF16)
            Wn_sb = sbA("Wn_sb", [128, 2 * 64], BF16)
            Sstg = sbA("Sstg", [128, 2 * 64], BF16)
            scg_st = sbA("scg_st", [128, 192], F32)
            attn = sbA("attn", [128, 8 * 128], BF16)
            out_sb = sbA("out_sb", [128, 2 * D], BF16)
            wstage = sbA("wstage", [128, 2 * D], F32)
            wout = sbA("wout", [128, 2 * D], BF16)
            ternA = sbA("ternA", [128, D], BF16)
            ternB = sbA("ternB", [128, D], BF16)
            tsum = sbA("tsum", [128, D], BF16)

            sl_U = [Slot(em) for _ in range(4)]
            sl_B = [Slot(em) for _ in range(2)]
            sl_M = [Slot(em) for _ in range(2)]
            sl_F = [Slot(em) for _ in range(2)]
            sl_W = [Slot(em) for _ in range(2)]
            sl_Sst = [Slot(em) for _ in range(2)]
            sl_x = [Slot(em) for _ in range(2)]
            sl_out = [Slot(em) for _ in range(2)]

            tok_w = {}
            for n in wnames:
                src = dws[n].ap().rearrange("(c p) n -> p c n", p=128)
                dst = W[n].ap().rearrange("p (c n) -> p c n", c=8)
                dma("gpsimd", dst, src, "QG")
            tok_wall = em.tok("QG")
            for n in wnames:
                tok_w[n] = tok_wall
            em.q["gpsimd"].append(("w", "QG", tok_wall[1]))
            em.hi["gpsimd"]["QG"] = tok_wall[1]

            def w1_slice(s):
                return dw1[128 * s:128 * (s + 1), :]

            def w2_slice(s):
                r, c = divmod(s, 4)
                return dw2[128 * r:128 * (r + 1), 1024 * c:1024 * (c + 1)]

            def w1_scr_slice(s):
                return dw1s[128 * s:128 * (s + 1), :]

            def w2_scr_slice(s):
                r, c = divmod(s, 4)
                return dw2s[128 * r:128 * (r + 1), 1024 * c:1024 * (c + 1)]

            slot_tok = [None, None]
            out_tok = [None, None]
            add_tok = [None]
            tern_units = []

            def _t1_load(which, s_, par):
                cell = [None]

                def f():
                    stg = wstage[:, par * D:(par + 1) * D]
                    em.wait("gpsimd", slot_tok[par])
                    cell[0] = dma("gpsimd", stg,
                                  w1_slice(s_) if which == 0 else w2_slice(s_),
                                  "QL%d" % par)
                return f, cell

            def _t1_comp(which, s_, par, cell):
                def f():
                    stg = wstage[:, par * D:(par + 1) * D]
                    em.wait("vector", cell[0])
                    col = which * 32 + s_
                    slot_tok[par] = em.op(
                        "vector",
                        lambda e, stg=stg, col=col: e.reduce_sum(
                            accm2[:, col:col + 1], stg, mybir.AxisListType.X,
                            apply_absolute_value=True),
                        inc="DVE")
                return f

            for i_, ws_ in enumerate([(0, s_) for s_ in range(32)] + [(1, s_) for s_ in range(32)]):
                lf, cell = _t1_load(ws_[0], ws_[1], i_ % 2)
                tern_units.append((lf, _t1_comp(ws_[0], ws_[1], i_ % 2, cell)))
            tok_thr = [None]

            def _thr_block():
                em.drain("vector")
                tz = em.op("vector", lambda e: e.memset(ttmp[:, :], 0.0), inc="DVE")
                tz2 = em.op("vector", lambda e: e.memset(rowst[:, :], 0.0), inc="DVE")
                em.drain("vector")
                em.op("vector", lambda e: e.reduce_sum(ttmp[:, 0:1], accm2[:, 0:32],
                                                       mybir.AxisListType.X), inc="DVE")
                em.op("vector", lambda e: e.reduce_sum(ttmp[:, 1:2], accm2[:, 32:64],
                                                       mybir.AxisListType.X), inc="DVE")
                em.drain("vector")
                for b in range(4):
                    em.op("vector",
                          lambda e, b=b: e.transpose(ttr2[0:32, 32 * b:32 * b + 32],
                                                     ttmp[32 * b:32 * b + 32, 0:32]), inc="DVE")
                em.drain("vector")
                em.op("vector", lambda e: e.reduce_sum(scs2[:, 0:1], ttr2[0:2, :],
                                                       mybir.AxisListType.X), inc="DVE")
                em.drain("vector")
                em.op("vector", lambda e: e.tensor_scalar(scs2[:, 1:2], scs2[:, 0:1],
                                                          1.0 / (FF * D), None, ALU.mult), inc="DVE")
                em.drain("vector")
                em.op("vector", lambda e: e.tensor_scalar(scs2[:, 2:3], scs2[:, 1:2],
                                                          1e-8, None, ALU.max), inc="DVE")
                em.drain("vector")
                em.op("vector", lambda e: e.tensor_scalar(scs2[:, 3:4], scs2[:, 2:3],
                                                          0.5, None, ALU.mult), inc="DVE")
                em.op("vector", lambda e: e.tensor_scalar(scs2[:, 4:5], scs2[:, 2:3],
                                                          -0.5, None, ALU.mult), inc="DVE")
                em.drain("vector")
                for j, col in enumerate((3, 4)):
                    em.op("vector",
                          lambda e, col=col: e.tensor_scalar(
                              rowst[0:2, 0:128], rowst[0:2, 0:128], 0.0,
                              scs2[:, col:col + 1], ALU.mult, ALU.add), inc="DVE")
                    em.drain("vector")
                    for b in range(4):
                        em.op("vector",
                              lambda e, j=j, b=b: e.transpose(
                                  colst[32 * b:32 * b + 32, 32 * j:32 * j + 32],
                                  rowst[0:32, 32 * b:32 * b + 32]), inc="DVE")
                    em.drain("vector")
                tok_thr[0] = em.tok("DVE")


            tern_units.append((None, _thr_block))

            def _t2_comp(which, s_, par, cell):
                def f():
                    stg = wstage[:, par * D:(par + 1) * D]
                    wo_ = wout[:, par * D:(par + 1) * D]
                    em.wait("scalar", cell[0])
                    em.wait("scalar", tok_thr[0])
                    em.wait("scalar", add_tok[0])
                    cp, cn = (0, 32) if which == 0 else (1, 33)
                    em.op("scalar",
                          lambda e, stg=stg, cn=cn: e.sign(ternA[:, :], stg,
                                                           bias=colst[:, cn:cn + 1]),
                          inc="ACT")
                    tsg = em.op("scalar",
                                lambda e, stg=stg, cp=cp: e.sign(ternB[:, :], stg,
                                                                 bias=colst[:, cp:cp + 1]),
                                inc="ACT")
                    em.wait("vector", tsg)
                    add_tok[0] = em.op(
                        "vector", lambda e: e.tensor_add(tsum[:, :], ternA[:, :], ternB[:, :]),
                        inc="DVE")
                    em.drain("vector")
                    em.wait("vector", out_tok[par])
                    slot_tok[par] = em.op(
                        "vector", lambda e, wo_=wo_: e.tensor_scalar_mul(wo_, tsum[:, :], 0.5),
                        inc="DVE")
                    em.drain("vector")
                    em.wait("gpsimd", slot_tok[par])
                    out_tok[par] = dma("gpsimd",
                                       w1_scr_slice(s_) if which == 0 else w2_scr_slice(s_),
                                       wo_, "QG%d" % par)
                return f

            for i_, ws_ in enumerate([(0, s_) for s_ in range(32)] + [(1, s_) for s_ in range(32)]):
                lf, cell = _t1_load(ws_[0], ws_[1], i_ % 2)
                tern_units.append((lf, _t2_comp(ws_[0], ws_[1], i_ % 2, cell)))

            tern_blocks = []
            _prev_comp = [None]

            def _mk_tb(lf, pc):
                def f():
                    if lf is not None:
                        lf()
                    if pc is not None:
                        pc()
                return f

            for lf, cf in tern_units:
                tern_blocks.append(_mk_tb(lf, _prev_comp[0]))
                _prev_comp[0] = cf
            tern_blocks.append(_mk_tb(None, _prev_comp[0]))

            tok_xnfm = [None] * 2
            tok_sp = [None, None]
            tok_exps = [None]
            tok_prep = None
            tok_S = [tok_states0] * 8
            tok_C = [tok_states0] * 8
            tok_G = [tok_states0] * 8
            tok_qk = [None]
            tok_v = [None, None]
            tok_state_dve = None
            tok_scan_pe = [None]
            pfc = [0]

            def pf_acquire():
                k = pfc[0] % 2
                pfc[0] += 1
                bk = bF[k]
                bk.acquire()
                bk.start()
                return bk, bFoff[k]

            def tp128(src, dst, wtoks=()):
                bk, off = pf_acquire()
                em.waits("tensor", list(wtoks))
                reg = pF[:, off:off + 128]
                em.op("tensor", lambda e, src=src, reg=reg: e.matmul(
                    reg, src, identb[:, :], start=True, stop=True,
                    skip_group_check=True))
                tpe = em.last_inc("tensor", "PE")
                em.wait("scalar", tpe)
                te = em.op("scalar", lambda e, dst=dst, reg=reg: e.activation(
                    dst, reg, AF.Copy), inc="ACT")
                bk.consumed(te)
                return te

            def ln_pre(cc):
                par = cc % 2
                xc = x_sb[:, par * D:(par + 1) * D]
                sl_x[par].wait_free("sync")
                tx = dma("sync", xc, dx[L * cc:L * (cc + 1), :], "QX%d" % par)
                em.wait("vector", tx)
                c0 = 4 * par
                em.op("vector",
                      lambda e, xc=xc, c0=c0: e.reduce_sum(
                          lntmp[:, c0:c0 + 1], xc, mybir.AxisListType.X), inc="DVE")
                em.op("vector",
                      lambda e, xc=xc, c0=c0: e.scalar_tensor_tensor(
                          sqsc[:, :], xc, 1.0, xc, ALU.mult, ALU.mult,
                          accum_out=lntmp[:, c0 + 1:c0 + 2]), inc="DVE")
                em.drain("vector")
                em.op("vector",
                      lambda e, cc=cc, c0=c0: e.tensor_scalar_mul(
                          lnkeep[:, 2 * cc:2 * cc + 1], lntmp[:, c0:c0 + 1], 1.0 / D), inc="DVE")
                em.drain("vector")
                em.op("vector",
                      lambda e, cc=cc, c0=c0: e.tensor_tensor(
                          lntmp[:, c0 + 2:c0 + 3], lnkeep[:, 2 * cc:2 * cc + 1],
                          lnkeep[:, 2 * cc:2 * cc + 1], ALU.mult), inc="DVE")
                em.drain("vector")
                tvar = em.op("vector",
                             lambda e, c0=c0: e.scalar_tensor_tensor(
                                 lntmp[:, c0 + 3:c0 + 4], lntmp[:, c0 + 1:c0 + 2], 1.0 / D,
                                 lntmp[:, c0 + 2:c0 + 3], ALU.mult, ALU.subtract), inc="DVE")
                em.wait("scalar", tvar)
                em.op("scalar",
                      lambda e, c0=c0: e.activation(
                          lntmp[:, c0 + 2:c0 + 3], lntmp[:, c0 + 3:c0 + 4], AF.Ln,
                          bias=cbias[:, 1:2]),
                      inc="ACT")
                em.drain("scalar")
                tr = em.op("scalar",
                           lambda e, cc=cc, c0=c0: e.activation(
                               lnkeep[:, 2 * cc + 1:2 * cc + 2], lntmp[:, c0 + 2:c0 + 3],
                               AF.Exp, scale=-0.5), inc="ACT")
                em.wait("vector", tr)
                txn = em.op("vector",
                            lambda e, xc=xc, cc=cc, par=par: e.tensor_scalar(
                                xn_tm[:, par * D:(par + 1) * D], xc,
                                lnkeep[:, 2 * cc:2 * cc + 1], lnkeep[:, 2 * cc + 1:2 * cc + 2],
                                ALU.subtract, ALU.mult), inc="DVE")
                sl_x[par].read_by(txn)
                return txn

            def ln_tp_fillers(cc, txn):
                par = cc % 2
                blocks = []
                for j in range(8):
                    def f(j=j, par=par, txn=txn):
                        te = tp128(xn_tm[:, par * D + 128 * j:par * D + 128 * (j + 1)],
                                   xn_fm[:, 256 * j + 128 * par:256 * j + 128 * (par + 1)],
                                   wtoks=(txn, tok_consts))
                        tok_xnfm[par] = te
                    blocks.append(f)
                return blocks

            def qk_fillers(ti):
                blocks = []
                for wn, dst in (("wq", Q_fm), ("wk", K_fm)):
                    for j in range(8):
                        def f(wn=wn, dst=dst, j=j):
                            bk, off = pf_acquire()
                            em.waits("tensor", [tok_xnfm[0], tok_xnfm[1], tok_w[wn]])
                            pap = pF[:, off:off + 256]
                            for k in range(8):
                                em.op("tensor",
                                      lambda e, wn=wn, j=j, k=k, pap=pap: e.matmul(
                                          pap, W[wn][:, D * k + 128 * j:D * k + 128 * (j + 1)],
                                          xn_fm[:, 256 * k:256 * (k + 1)],
                                          start=(k == 0), stop=(k == 7),
                                          skip_group_check=True))
                            tmm = em.last_inc("tensor", "PE")
                            em.wait("scalar", tmm)
                            tev = em.op("scalar",
                                        lambda e, dst=dst, j=j, pap=pap: e.activation(
                                            dst.ap().rearrange("p (c n) -> p c n", c=2)
                                            [:, :, 128 * j:128 * (j + 1)],
                                            pap.rearrange("p (c n) -> p c n", c=2),
                                            AF.Copy),
                                        inc="ACT")
                            bk.consumed(tev)
                            tok_qk[0] = tev
                        blocks.append(f)
                return blocks

            def vgcum_fillers(cc):
                par = cc % 2

                def xnsl(k, par=par):
                    return xn_fm[:, 256 * k + 128 * par:256 * k + 128 * (par + 1)]

                blocks = []
                for h2 in range(2):
                    def fv(h2=h2, par=par):
                        bk, off = pf_acquire()
                        em.waits("tensor", [tok_w["wv"], tok_xnfm[par]])
                        pap = pF[:, off:off + 512]
                        for k in range(8):
                            em.op("tensor",
                                  lambda e, k=k, h2=h2, pap=pap, a=xnsl(k): e.matmul(
                                      pap, a, W["wv"][:, D * k + 512 * h2:D * k + 512 * h2 + 512],
                                      start=(k == 0), stop=(k == 7), skip_group_check=True))
                        tmm = em.last_inc("tensor", "PE")
                        em.wait("scalar", tmm)
                        tev = em.op("scalar",
                                    lambda e, h2=h2, par=par, pap=pap: e.activation(
                                        V_tm[:, par * D + 512 * h2:par * D + 512 * h2 + 512],
                                        pap, AF.Copy), inc="ACT")
                        bk.consumed(tev)
                        tok_v[par] = tev
                    blocks.append(fv)
                for gate, (wn, bhi, blo, spdst) in enumerate(
                        (("wgk", bgk_hi, bgk_lo, spA), ("wgc", bgc_hi, bgc_lo, spR))):
                    for h2 in range(2):
                        def fg(gate=gate, wn=wn, bhi=bhi, blo=blo, spdst=spdst, h2=h2, par=par):
                            bk, off = pf_acquire()
                            em.waits("tensor", [tok_w[wn], tok_xnfm[par]])
                            pap = pF[:, off:off + 512]
                            em.op("tensor",
                                  lambda e, bhi=bhi, h2=h2, pap=pap: e.matmul(
                                      pap, onesr[:, :], bhi[:, 512 * h2:512 * h2 + 512],
                                      start=True, stop=False, skip_group_check=True))
                            em.op("tensor",
                                  lambda e, blo=blo, h2=h2, pap=pap: e.matmul(
                                      pap, onesr[:, :], blo[:, 512 * h2:512 * h2 + 512],
                                      start=False, stop=False, skip_group_check=True))
                            for k in range(8):
                                em.op("tensor",
                                      lambda e, wn=wn, k=k, h2=h2, pap=pap, a=xnsl(k): e.matmul(
                                          pap, a, W[wn][:, D * k + 512 * h2:D * k + 512 * h2 + 512],
                                          start=False, stop=(k == 7), skip_group_check=True))
                            tmm = em.last_inc("tensor", "PE")
                            em.wait("scalar", tmm)
                            em.drain("scalar")
                            em.op("scalar",
                                  lambda e, pap=pap: e.activation(
                                      e_scr[:, 0:512], pap, AF.Exp, scale=-1.0),
                                  inc="ACT")
                            em.drain("scalar")
                            tev = em.op("scalar",
                                        lambda e, h2=h2, spdst=spdst: e.activation(
                                            spdst[:, 512 * h2:512 * h2 + 512],
                                            e_scr[:, 0:512], AF.Ln,
                                            bias=cbias[:, 0:1]),
                                        inc="ACT")
                            bk.consumed(tev)
                            tok_sp[gate] = tev
                        blocks.append(fg)
                for gate, (sp, Edst, Idst) in enumerate(((spA, EAx, IAx), (spR, ERx, IRx))):
                    def fc(gate=gate, sp=sp, Edst=Edst, Idst=Idst):
                        bF[0].acquire()
                        bF[0].start()
                        bF[1].acquire()
                        bF[1].start()
                        em.wait("tensor", tok_sp[gate])
                        for j in range(8):
                            em.op("tensor",
                                  lambda e, sp=sp, j=j, st=(j in (0, 4)): e.matmul(
                                      pF[:, 128 * j:128 * (j + 1)], sp[:, 128 * j:128 * (j + 1)],
                                      maski[:, :], start=st, stop=True, skip_group_check=True))
                        tmm = em.last_inc("tensor", "PE")
                        em.wait("scalar", tmm)
                        em.op("scalar",
                              lambda e, Edst=Edst: e.activation(Edst[:, :], pF[:, :],
                                                                AF.Exp, scale=-1.0),
                              inc="ACT")
                        tev = em.op("scalar",
                                    lambda e, Idst=Idst: e.activation(Idst[:, :], pF[:, :],
                                                                      AF.Exp, scale=1.0),
                                    inc="ACT")
                        bF[0].consumed(tev)
                        bF[1].consumed(tev)
                        tok_exps[0] = tev
                    blocks.append(fc)
                return blocks

            def wo_dense(cc, tok_attn):
                par = cc % 2
                toks = []
                for h2 in range(2):
                    bk, off = pf_acquire()
                    em.waits("tensor", [tok_attn, tok_w["wo"]])
                    pap = pF[:, off:off + 512]
                    for k in range(8):
                        em.op("tensor",
                              lambda e, k=k, h2=h2, pap=pap: e.matmul(
                                  pap, attn[:, 128 * k:128 * (k + 1)],
                                  W["wo"][:, D * k + 512 * h2:D * k + 512 * h2 + 512],
                                  start=(k == 0), stop=(k == 7), skip_group_check=True))
                    tmm = em.last_inc("tensor", "PE")
                    em.wait("scalar", tmm)
                    sl_out[par].wait_free("scalar")
                    tev = em.op("scalar",
                                lambda e, par=par, h2=h2, pap=pap: e.activation(
                                    out_sb[:, par * D + 512 * h2:par * D + 512 * h2 + 512],
                                    pap, AF.Copy), inc="ACT")
                    bk.consumed(tev)
                tok_o = em.tok("ACT")
                em.wait("sync", tok_o)
                tdo = dma("sync", dyp[L * cc:L * (cc + 1), :], out_sb[:, par * D:(par + 1) * D],
                          "QO%d" % par)
                sl_out[par].read_by(tdo)

            def prep_dense(cc, wo_prev=None):
                nonlocal tok_prep, tok_state_dve
                par = cc % 2
                em.drain("scalar")
                em.wait("scalar", tok_state_dve)
                em.op("scalar",
                      lambda e: e.activation(
                          EAst[:, :],
                          EAx.ap().rearrange("p (b t) -> p b t", t=128)[:, :, 127],
                          AF.Copy), inc="ACT")
                tea = em.op("scalar",
                            lambda e: e.activation(
                                ERst[:, :],
                                ERx.ap().rearrange("p (b t) -> p b t", t=128)[:, :, 127],
                                AF.Copy), inc="ACT")
                em.drain("scalar")
                qsl = Q_fm[:, par * D:(par + 1) * D]
                ksl = K_fm[:, par * D:(par + 1) * D]
                em.waits("vector", [tok_qk[0], tok_exps[0], tok_scan_pe[0]])
                em.op("vector", lambda e, ksl=ksl: e.tensor_mul(Kps[:, :], ksl, IAx[:, :]), inc="DVE")
                tok_kz0 = em.op("vector", lambda e, qsl=qsl: e.tensor_mul(Zs[:, :], qsl, IRx[:, :]),
                                inc="DVE")
                em.op("vector", lambda e, qsl=qsl: e.tensor_mul(Xs[:, :], qsl, EAx[:, :]), inc="DVE")
                em.op("vector",
                      lambda e, ksl=ksl: e.scalar_tensor_tensor(Kpns[:, :], ksl, -1.0, ERx[:, :],
                                                                ALU.mult, ALU.mult), inc="DVE")
                tmul = em.op("vector", lambda e: e.tensor_mul(PRm[:, :], EAx[:, :], ERx[:, :]),
                             inc="DVE")
                tok_prep = tmul
                if wo_prev is not None:
                    wo_dense(*wo_prev)
                ttm = None
                for src, dst in ((Kps, Kptm), (Zs, Ztm)):
                    for j in range(8):
                        ttm = tp128(src[:, 128 * j:128 * (j + 1)], dst[:, 128 * j:128 * (j + 1)],
                                    wtoks=(tok_kz0,))
                return ttm

            def scan_chunk(cc, tok_tm, fillers):
                nonlocal tok_state_dve
                par = cc % 2
                fillers = list(fillers)
                fi = [0]
                nper = max(1, (len(fillers) + 47) // 48)

                def drain(last=False):
                    n = len(fillers) if last else min(nper, len(fillers) - fi[0])
                    while n > 0 and fi[0] < len(fillers):
                        fillers[fi[0]]()
                        fi[0] += 1
                        n -= 1

                TFap = pL[:, 512:640]
                Oap = pL[:, 640:768]
                STap = pL[:, 0:64]
                CCap = pL[:, 64:128]
                GGap = pL[:, 128:192]
                em.waits("tensor", [tok_tm, tok_v[par], tok_prep])
                for p in range(8):
                    co = 128 * p
                    hU = [sl_U[(2 * p) % 4], sl_U[(2 * p + 1) % 4]]
                    uap = [U_sb[:, 128 * ((2 * p) % 4):128 * ((2 * p) % 4) + 128],
                           U_sb[:, 128 * ((2 * p + 1) % 4):128 * ((2 * p + 1) % 4) + 128]]
                    bsap = [Bs_sb[:, 0:128], Bs_sb[:, 128:256]]
                    mtap = [Mt_sb[:, 0:128], Mt_sb[:, 128:256]]
                    fap = F_sb[:, 128 * (p % 2):128 * (p % 2) + 128]
                    wnap = [Wn_sb[:, 0:64], Wn_sb[:, 64:128]]
                    ssap = Sstg[:, 64 * (p % 2):64 * (p % 2) + 64]
                    Xh = [Xs[0:64, co:co + 128], Xs[64:128, co:co + 128]]
                    Kph = [Kps[0:64, co:co + 128], Kps[64:128, co:co + 128]]
                    Zh = [Zs[0:64, co:co + 128], Zs[64:128, co:co + 128]]
                    Knh = [Kpns[0:64, co:co + 128], Kpns[64:128, co:co + 128]]
                    Ssl = [S_slc(p, 0), S_slc(p, 1)]
                    Csl = [C_slc(p, 0), C_slc(p, 1)]
                    Gsl = [G_slc(p, 0), G_slc(p, 1)]
                    Ktmh = [Kptm[:, 64 * 2 * p:64 * 2 * p + 64],
                            Kptm[:, 64 * (2 * p + 1):64 * (2 * p + 1) + 64]]
                    Ztmh = [Ztm[:, 64 * 2 * p:64 * 2 * p + 64],
                            Ztm[:, 64 * (2 * p + 1):64 * (2 * p + 1) + 64]]
                    Vh = [V_tm[:, par * D + 64 * 2 * p:par * D + 64 * 2 * p + 64],
                          V_tm[:, par * D + 64 * (2 * p + 1):par * D + 64 * (2 * p + 1) + 64]]
                    idb = [identb[0:64, 0:64], identb[64:128, 64:128]]
                    idf = [identf[0:64, 0:64], identf[64:128, 64:128]]

                    for q in range(2):
                        bk = bA[q]
                        bk.acquire()
                        em.op("tensor",
                              lambda e, q=q, bk=bk, a=Kph[q], b=Xh[q]: e.matmul(
                                  bk.ap[:, 0:128], a, b,
                                  start=True, stop=True, tile_position=(64 * q, 0),
                                  skip_group_check=True))
                    tU = em.last_inc("tensor", "PE")
                    for q in range(2):
                        bk = bB[q]
                        bk.acquire()
                        em.op("tensor",
                              lambda e, q=q, bk=bk, a=Zh[q], b=Knh[q]: e.matmul(
                                  bk.ap[:, 0:128], a, b,
                                  start=True, stop=True, tile_position=(64 * q, 0),
                                  skip_group_check=True))
                    tB = em.last_inc("tensor", "PE")
                    drain()
                    em.wait("vector", tU)
                    for q in range(2):
                        hU[q].wait_free("vector")
                        tm_ = em.op("vector",
                                    lambda e, q=q, o=uap[q]: e.tensor_tensor(
                                        o, bA[q].ap[:, 0:128], maski[:, :], ALU.mult),
                                    inc="DVE")
                        bA[q].consumed(tm_)
                    tok_umask = em.tok("DVE")
                    em.wait("vector", tB)
                    for q in range(2):
                        sl_B[q].wait_free("vector")
                        tm_ = em.op("vector",
                                    lambda e, q=q, o=bsap[q]: e.tensor_tensor(
                                        o, bB[q].ap[:, 0:128], masks[:, :], ALU.mult),
                                    inc="DVE")
                        bB[q].consumed(tm_)
                    tok_bmask = em.tok("DVE")
                    bL1.acquire()
                    bL1.start()
                    em.waits("tensor", [tok_S[p], tok_umask])
                    for q in range(2):
                        em.op("tensor",
                              lambda e, q=q, a=Ssl[q], b=Xh[q]: e.matmul(
                                  TFap[64 * q:64 * (q + 1), :], a, b,
                                  start=True, stop=False, tile_position=(64 * q, 64 * q),
                                  skip_group_check=True))
                        em.op("tensor",
                              lambda e, q=q, a=Ktmh[q], b=uap[q]: e.matmul(
                                  TFap[64 * q:64 * (q + 1), :], a, b,
                                  start=False, stop=True,
                                  tile_position=(0, 64 * q), skip_group_check=True))
                    tTF = em.last_inc("tensor", "PE")
                    em.wait("vector", tTF)
                    sl_F[p % 2].wait_free("vector")
                    tF = em.op("vector",
                               lambda e, fap=fap, co=co: e.tensor_tensor(
                                   fap, TFap, PRm[:, co:co + 128], ALU.mult), inc="DVE")
                    drain()
                    em.wait("tensor", tF)
                    for q in range(2):
                        bk = bA[q]
                        bk.acquire()
                        em.op("tensor",
                              lambda e, q=q, bk=bk, a=Zh[q], b=fap[64 * q:64 * (q + 1), :]:
                              e.matmul(bk.ap[:, 0:128], a, b,
                                       start=True, stop=True, tile_position=(64 * q, 0),
                                       skip_group_check=True))
                    tMt = em.last_inc("tensor", "PE")
                    em.wait("vector", tMt)
                    for q in range(2):
                        sl_M[q].wait_free("vector")
                        tm_ = em.op("vector",
                                    lambda e, q=q, o=mtap[q]: e.tensor_tensor(
                                        o, bA[q].ap[:, 0:128], maski[:, :], ALU.mult),
                                    inc="DVE")
                        bA[q].consumed(tm_)
                    tok_mmask = em.tok("DVE")
                    drain()
                    em.waits("tensor", [tok_C[p], tok_bmask])
                    for q in range(2):
                        bk = bB[q]
                        bk.acquire()
                        em.op("tensor",
                              lambda e, q=q, bk=bk, a=Knh[q], b=Csl[q]: e.matmul(
                                  bk.ap[:, 0:64], a, b,
                                  start=True, stop=False, tile_position=(64 * q, 0),
                                  skip_group_check=True))
                        em.op("tensor",
                              lambda e, q=q, bk=bk, a=bsap[q], b=Vh[q]: e.matmul(
                                  bk.ap[:, 0:64], a, b,
                                  start=False, stop=True, tile_position=(0, 0),
                                  skip_group_check=True))
                    tW = em.last_inc("tensor", "PE")
                    em.wait("scalar", tW)
                    for q in range(2):
                        sl_W[q].wait_free("scalar")
                        te = em.op("scalar",
                                   lambda e, q=q, o=wnap[q]: e.activation(
                                       o, bB[q].ap[:, 0:64], AF.Copy), inc="ACT")
                        bB[q].consumed(te)
                    tok_wn = em.tok("ACT")
                    drain()
                    em.waits("tensor", [tok_G[p], tok_mmask, tok_wn, tF])
                    for q in range(2):
                        em.op("tensor",
                              lambda e, q=q, a=Vh[q], b=mtap[q]: e.matmul(
                                  Oap[64 * q:64 * (q + 1), :], a, b,
                                  start=False, stop=False, tile_position=(0, 64 * q),
                                  skip_group_check=True))
                        em.op("tensor",
                              lambda e, q=q, a=Csl[q], b=fap[64 * q:64 * (q + 1), :]: e.matmul(
                                  Oap[64 * q:64 * (q + 1), :], a, b,
                                  start=False, stop=False, tile_position=(64 * q, 64 * q),
                                  skip_group_check=True))
                        em.op("tensor",
                              lambda e, q=q, a=Gsl[q], b=Xh[q]: e.matmul(
                                  Oap[64 * q:64 * (q + 1), :], a, b,
                                  start=False, stop=False, tile_position=(64 * q, 64 * q),
                                  skip_group_check=True))
                        em.op("tensor",
                              lambda e, q=q, a=wnap[q], b=uap[q]: e.matmul(
                                  Oap[64 * q:64 * (q + 1), :], a, b,
                                  start=False, stop=True, tile_position=(0, 64 * q),
                                  skip_group_check=True))
                    tO = em.last_inc("tensor", "PE")
                    for q in range(2):
                        hU[q].read_by(tO)
                        sl_B[q].read_by(tO)
                        sl_M[q].read_by(tO)
                    sl_F[p % 2].read_by(tO)
                    em.wait("scalar", tO)
                    tOe = em.op("scalar",
                                lambda e, co=co: e.activation(
                                    attn[:, co:co + 128], Oap, AF.Copy), inc="ACT")
                    drain()
                    bL0.acquire()
                    bL0.start()
                    em.waits("tensor", [tok_S[p], tok_C[p], tok_G[p]])
                    for q in range(2):
                        em.op("tensor",
                              lambda e, q=q, a=Ktmh[q]: e.matmul(
                                  STap[64 * q:64 * (q + 1), :], a, a,
                                  start=True, stop=True,
                                  tile_position=(0, 64 * q), skip_group_check=True))
                        em.op("tensor",
                              lambda e, q=q, a=Ztmh[q], b=Vh[q]: e.matmul(
                                  CCap[64 * q:64 * (q + 1), :], a, b,
                                  start=False, stop=True, tile_position=(0, 64 * q),
                                  skip_group_check=True))
                        em.op("tensor",
                              lambda e, q=q, a=Ktmh[q], b=wnap[q]: e.matmul(
                                  GGap[64 * q:64 * (q + 1), :], a, b,
                                  start=False, stop=True, tile_position=(0, 64 * q),
                                  skip_group_check=True))
                    tST = em.last_inc("tensor", "PE")
                    for q in range(2):
                        sl_W[q].read_by(tST)
                    em.wait("vector", tST)
                    sl_Sst[p % 2].wait_free("vector")
                    em.op("vector",
                          lambda e, p=p: e.tensor_add(
                              scg_st[:, :], pL[:, 0:192],
                              SCG[:, 192 * p:192 * p + 192]), inc="DVE")
                    em.drain("vector")
                    tSs = em.op("vector",
                                lambda e, ssap=ssap, p=p: e.tensor_scalar(
                                    ssap, scg_st[:, 0:64], EAst[:, p:p + 1], None,
                                    ALU.mult), inc="DVE")
                    tCs = em.op("vector",
                                lambda e, p=p: e.tensor_scalar(
                                    SCG[:, 192 * p + 64:192 * p + 128], scg_st[:, 64:128],
                                    ERst[:, p:p + 1], None, ALU.mult), inc="DVE")
                    tGs = em.op("vector",
                                lambda e, p=p: e.tensor_scalar(
                                    SCG[:, 192 * p + 128:192 * p + 192], scg_st[:, 128:192],
                                    EAst[:, p:p + 1], None, ALU.mult), inc="DVE")
                    tok_C[p] = tCs
                    tok_G[p] = tGs
                    tok_state_dve = tGs
                    bL0.consumed(tGs)
                    em.drain("vector")
                    drain()
                    em.wait("tensor", tSs)
                    bA[1].acquire()
                    bA[1].start()
                    for q in range(2):
                        em.op("tensor",
                              lambda e, q=q, a=ssap[64 * q:64 * (q + 1), :], i=idb[q]:
                              e.matmul(bA[1].ap[64 * q:64 * (q + 1), 0:64], a, i,
                                       start=True, stop=True,
                                       tile_position=(64 * q, 64 * q),
                                       skip_group_check=True), inc=None)
                    tTr = em.last_inc("tensor", "PE")
                    sl_Sst[p % 2].read_by(tTr)
                    em.wait("scalar", tTr)
                    tSe = None
                    for q in range(2):
                        tSe = em.op("scalar",
                                    lambda e, q=q, p=p: e.activation(
                                        S_slc(p, q),
                                        bA[1].ap[64 * q:64 * (q + 1), 0:64], AF.Copy,
                                        scale=EAst[64 * q:64 * (q + 1), p:p + 1]), inc="ACT")
                    bA[1].consumed(tSe)
                    tok_S[p] = tSe
                    bL1.consumed(tOe)
                    drain()
                drain(last=True)
                tok_scan_pe[0] = em.tok("PE")
                tok_state_dve = tGs
                return em.tok("ACT")

            dbg_specs = [("xn_tm", xn_tm), ("xn_fm", xn_fm), ("Q_fm", Q_fm),
                         ("K_fm", K_fm), ("V_tm", V_tm), ("spA", spA), ("spR", spR),
                         ("EAx", EAx), ("IAx", IAx), ("IRx", IRx), ("ERx", ERx),
                         ("Xs", Xs), ("Kps", Kps), ("Zs", Zs), ("Kpns", Kpns),
                         ("PRm", PRm), ("Kptm", Kptm), ("Ztm", Ztm), ("U_sb", U_sb),
                         ("Bs_sb", Bs_sb), ("Mt_sb", Mt_sb), ("F_sb", F_sb),
                         ("Wn_sb", Wn_sb), ("attn", attn), ("SCG", SCG),
                         ("EAst", EAst), ("ERst", ERst)]
            if debug:
                dbg_dr = {nm: nc.dram_tensor("dbg_" + nm, list(t.shape), t.dtype,
                                             kind="ExternalOutput")
                          for nm, t in dbg_specs}

            def dbg_dump():
                for sem_ in ("PE", "DVE", "ACT"):
                    em.wait("sync", em.tok(sem_))
                for nm, t in dbg_specs:
                    dma("sync", dbg_dr[nm][:, :], t[:, :], "QC")
                em.q["sync"].append(("w", "QC", em.cnt["QC"]))
                em.hi["sync"]["QC"] = em.cnt["QC"]
                for eng_ in ("vector", "scalar", "tensor", "gpsimd"):
                    em.wait(eng_, ("QC", em.cnt["QC"]))

            t0 = ln_pre(0)
            t1 = ln_pre(1)
            for f in ln_tp_fillers(0, t0):
                f()
            for f in ln_tp_fillers(1, t1):
                f()
            for f in qk_fillers(0):
                f()
            for f in vgcum_fillers(0):
                f()
            pending_wo = None
            tern_i = [0]
            tern_per = (len(tern_blocks) + NT - 1) // NT

            def tern_drain(n):
                while n > 0 and tern_i[0] < len(tern_blocks):
                    tern_blocks[tern_i[0]]()
                    tern_i[0] += 1
                    n -= 1

            def tern_take(n):
                out = []
                while n > 0 and tern_i[0] < len(tern_blocks):
                    out.append(tern_blocks[tern_i[0]])
                    tern_i[0] += 1
                    n -= 1
                return out

            for ti in range(NT):
                c0, c1 = 2 * ti, 2 * ti + 1
                ttm0 = prep_dense(c0, wo_prev=pending_wo)
                fill0 = vgcum_fillers(c1) + tern_take(tern_per // 2 + 1)
                ta0 = scan_chunk(c0, ttm0, fill0)
                ttm1 = prep_dense(c1, wo_prev=(c0, ta0))
                fill1 = []
                if ti + 1 < NT:
                    tx0 = ln_pre(c0 + 2)
                    tx1 = ln_pre(c1 + 2)
                    fill1 += ln_tp_fillers(c0 + 2, tx0) + ln_tp_fillers(c1 + 2, tx1)
                    fill1 += qk_fillers(ti + 1) + vgcum_fillers(c0 + 2)
                fill1 += tern_take(tern_per // 2 + 1)
                ta1 = scan_chunk(c1, ttm1, fill1)
                pending_wo = (c1, ta1)
                if debug and ti == 0:
                    dbg_dump()
            tern_drain(len(tern_blocks))
            wo_dense(*pending_wo)
            tok_tern_done = [em.tok("QG0"), em.tok("QG1")]

            tokA = {e: em.tok(s) for e, s in ENG_SEM.items()}
            tokA_q = [em.tok(q) for q in ("QC", "QX0", "QX1", "QO0", "QO1", "QG")]

            with nc.Block() as block:
                em.flush(block)

        with contextlib.ExitStack() as ctxB:
            sbB = lambda name, shape, dt: ctxB.enter_context(nc.sbuf_tensor(name, shape, dt))
            W1T = sbB("W1T", [128, 8 * FF], BF16)
            W2T = sbB("W2T", [128, 32 * D], BF16)
            x_sb = sbB("xB_sb", [128, 4 * D], F32)
            xn_tm = sbB("xnB_tm", [128, 2 * D], BF16)
            xn_fm = sbB("xnB_fm", [128, 2 * 8 * 256], BF16)
            h_fm = sbB("h_fm", [128, 32 * 256], BF16)
            yp_sb = sbB("yp_sb", [128, 2 * D], BF16)
            out_sb = sbB("outB_sb", [128, 2 * D], F32)

            allA = ([tokA["tensor"], tokA["vector"], tokA["scalar"], tokA["gpsimd"]]
                    + list(tok_tern_done) + list(tokA_q))
            for eng in ENGINES:
                em.waits(eng, allA)
                if eng in ENG_SEM:
                    em.drain(eng)

            def dmat(eng, out, in_, q):
                return em.op(eng,
                             lambda e, o=out, i=in_: e.dma_start(o, i, transpose=True),
                             inc=q, amt=16)

            for d_ in range(8):
                dmat("sync", W1T[:, FF * d_:FF * (d_ + 1)],
                     dw1s[:, 128 * d_:128 * (d_ + 1)], "QW")
            tok_w1b = em.tok("QW")
            for f_ in range(32):
                dmat("sync", W2T[:, D * f_:D * (f_ + 1)],
                     dw2s[:, 128 * f_:128 * (f_ + 1)], "QW2")
            tok_w2b = em.tok("QW2")

            sl_xB = [Slot(em) for _ in range(4)]
            sl_yp = [Slot(em) for _ in range(2)]
            sl_oB = [Slot(em) for _ in range(2)]

            tok_tpB = [None, None]
            tok_xB = {}

            def lnb_dma(cc):
                scc = cc % 4
                xc = x_sb[:, scc * D:(scc + 1) * D]
                sl_xB[scc].wait_free("sync")
                tok_xB[cc] = dma("sync", xc, dx[L * cc:L * (cc + 1), :],
                                 "QX%d" % (cc % 4))

            def lnb_ts(cc):
                par = cc % 2
                scc = cc % 4
                xc = x_sb[:, scc * D:(scc + 1) * D]
                em.wait("vector", tok_xB[cc])
                em.wait("vector", tok_tpB[par])
                txn = em.op("vector",
                            lambda e, xc=xc, cc=cc, par=par: e.tensor_scalar(
                                xn_tm[:, par * D:(par + 1) * D], xc,
                                lnkeep[:, 2 * cc:2 * cc + 1], lnkeep[:, 2 * cc + 1:2 * cc + 2],
                                ALU.subtract, ALU.mult), inc="DVE")
                return txn

            def lnb_tp_fillers(cc, txn):
                par = cc % 2
                tpar = (cc // 2) % 2
                blocks = []
                for j in range(8):
                    def f(j=j, par=par, tpar=tpar, txn=txn):
                        te = tp128(xn_tm[:, par * D + 128 * j:par * D + 128 * (j + 1)],
                                   xn_fm[:, 2048 * tpar + 256 * j + 128 * par:
                                         2048 * tpar + 256 * j + 128 * (par + 1)],
                                   wtoks=(txn,))
                        tok_tpB[par] = te
                    blocks.append(f)
                return blocks

            fb_banks = [bA[0], bA[1], bB[0]]
            for cc in range(min(4, CH)):
                lnb_dma(cc)
            t0 = lnb_ts(0)
            t1 = lnb_ts(1)
            for f in lnb_tp_fillers(0, t0):
                f()
            for f in lnb_tp_fillers(1, t1):
                f()
            for ti in range(NT):
                c0, c1 = 2 * ti, 2 * ti + 1
                tpar = ti % 2
                typ_tok = {}
                for cc in (c0, c1):
                    par = cc % 2
                    sl_yp[par].wait_free("sync")
                    typ_tok[cc] = dma("sync", yp_sb[:, par * D:(par + 1) * D],
                                      dyp[L * cc:L * (cc + 1), :], "QY%d" % par)
                fillersB = []
                if ti + 1 < NT:
                    tb0 = lnb_ts(c0 + 2)
                    tb1 = lnb_ts(c1 + 2)
                    fillersB = lnb_tp_fillers(c0 + 2, tb0) + lnb_tp_fillers(c1 + 2, tb1)
                fbi = [0]

                def drainB():
                    if fbi[0] < len(fillersB):
                        fillersB[fbi[0]]()
                        fbi[0] += 1

                em.waits("tensor", [tok_tpB[0], tok_tpB[1], tok_w1b])
                for fc in range(32):
                    bk = fb_banks[fc % 3]
                    bk.acquire()
                    pap = bk.ap[:, 0:256]
                    for k in range(8):
                        em.op("tensor",
                              lambda e, fc=fc, k=k, pap=pap, tpar=tpar: e.matmul(
                                  pap, W1T[:, FF * k + 128 * fc:FF * k + 128 * (fc + 1)],
                                  xn_fm[:, 2048 * tpar + 256 * k:2048 * tpar + 256 * (k + 1)],
                                  start=(k == 0), stop=(k == 7), skip_group_check=True))
                    tmm = em.last_inc("tensor", "PE")
                    em.wait("scalar", tmm)
                    te = em.op("scalar",
                               lambda e, fc=fc, pap=pap: e.activation(
                                   h_fm[:, 256 * fc:256 * (fc + 1)], pap, AF.Gelu,
                                   bias=b1_sb[:, fc:fc + 1]), inc="ACT")
                    bk.consumed(te)
                    if fc % 2 == 1:
                        drainB()
                tok_h = em.tok("ACT")
                for cc in (c0, c1):
                    par = cc % 2
                    ccr = cc - 2 * ti
                    typ = typ_tok[cc]
                    em.wait("tensor", tok_h)
                    em.wait("tensor", tok_w2b)
                    for h2 in range(2):
                        bk = (bL0, bL1)[h2]
                        bk.acquire()
                        pap = pL[:, 512 * h2:512 * (h2 + 1)]
                        em.op("tensor",
                              lambda e, h2=h2, pap=pap, st=bk.start(): e.matmul(
                                  pap, onesr[:, :], b2_hi[:, 512 * h2:512 * h2 + 512],
                                  start=st, stop=False, skip_group_check=True))
                        for fc in range(32):
                            em.op("tensor",
                                  lambda e, fc=fc, h2=h2, ccr=ccr, pap=pap: e.matmul(
                                      pap, h_fm[:, 256 * fc + 128 * ccr:256 * fc + 128 * ccr + 128],
                                      W2T[:, D * fc + 512 * h2:D * fc + 512 * h2 + 512],
                                      start=False, stop=(fc == 31), skip_group_check=True))
                        tmm = em.last_inc("tensor", "PE")
                        em.waits("vector", [tmm, typ])
                        sl_oB[par].wait_free("vector")
                        em.op("vector",
                              lambda e, par=par, h2=h2, pap=pap, scc=cc % 4: e.scalar_tensor_tensor(
                                  out_sb[:, par * D + 512 * h2:par * D + 512 * h2 + 512],
                                  pap, 1.0, x_sb[:, scc * D + 512 * h2:scc * D + 512 * h2 + 512],
                                  ALU.mult, ALU.add), inc="DVE")
                        em.drain("vector")
                        tev = em.op("vector",
                                    lambda e, par=par, h2=h2: e.tensor_add(
                                        out_sb[:, par * D + 512 * h2:par * D + 512 * h2 + 512],
                                        out_sb[:, par * D + 512 * h2:par * D + 512 * h2 + 512],
                                        yp_sb[:, par * D + 512 * h2:par * D + 512 * h2 + 512]),
                                    inc="DVE")
                        bk.consumed(tev)
                        drainB()
                    tok_oe = em.tok("DVE")
                    sl_xB[cc % 4].read_by(tok_oe)
                    sl_yp[par].read_by(tok_oe)
                    em.wait("sync", tok_oe)
                    tdo = dma("sync", dy[L * cc:L * (cc + 1), :], out_sb[:, par * D:(par + 1) * D],
                              "QO%d" % par)
                    sl_oB[par].read_by(tdo)
                    if cc + 4 < CH:
                        lnb_dma(cc + 4)
                while fbi[0] < len(fillersB):
                    fillersB[fbi[0]]()
                    fbi[0] += 1

            em.op("sync", lambda e: e.nop())
            for q in ("QO0", "QO1"):
                fin = em.tok(q)
                em.q["sync"].append(("w", q, fin[1]))
                em.hi["sync"][q] = fin[1]
            with nc.Block() as block:
                em.flush(block)

    return nc


_NC_CACHE = {}


def kernel(**inputs):
    T = 4096
    x = np.ascontiguousarray(inputs["L0_attn_x"], dtype=np.float32)
    base = {
        "wq": np.ascontiguousarray(inputs["L0_attn_W_q"], np.float32),
        "wk": np.ascontiguousarray(inputs["L0_attn_W_k"], np.float32),
        "wv": np.ascontiguousarray(inputs["L0_attn_W_v"], np.float32),
        "wgk": np.ascontiguousarray(inputs["L0_attn_Wg_K"], np.float32),
        "wgc": np.ascontiguousarray(inputs["L0_attn_Wg_C"], np.float32),
        "wo": np.ascontiguousarray(inputs["L0_attn_W_o"], np.float32),
        "w1": np.ascontiguousarray(inputs["L0_ffn_W1"], np.float32),
        "b1": np.ascontiguousarray(inputs["L0_ffn_b1"], np.float32),
        "w2": np.ascontiguousarray(inputs["L0_ffn_W2"], np.float32),
        "b2": np.ascontiguousarray(inputs["L0_ffn_b2"], np.float32),
        "bgk": np.ascontiguousarray(inputs["L0_attn_bg_K"], np.float32),
        "bgc": np.ascontiguousarray(inputs["L0_attn_bg_C"], np.float32),
        "identb": np.eye(128, dtype=BF),
        "identf": np.eye(128, dtype=np.float32),
        "maski": np.triu(np.ones((128, 128), np.float32)).astype(BF),
        "masks": np.triu(np.ones((128, 128), np.float32), 1).astype(BF),
        "onesrow": np.ones((1, 128), BF),
    }
    if "nc" not in _NC_CACHE:
        _NC_CACHE["nc"] = build(T)
    nc = _NC_CACHE["nc"]
    in_maps = [dict(base, x=x[b]) for b in range(8)]
    res = run_bass_kernel_spmd(nc, in_maps, core_ids=list(range(8)))
    return np.stack([np.asarray(r["y"], np.float32) for r in res.results])
